# version 10
# baseline (speedup 1.0000x reference)
import numpy as np
import ml_dtypes
import concourse.bass as bass
import concourse.mybir as mybir
from concourse.bass_utils import run_bass_kernel_spmd
from contextlib import ExitStack

F32 = mybir.dt.float32
BF16 = mybir.dt.bfloat16
I32 = mybir.dt.int32
AF = mybir.ActivationFunctionType
ALU = mybir.AluOpType
AX = mybir.AxisListType

T = 1024
D = 4096
NT = T // 128
NK = D // 128
NB = 256
NSA = D // NB
HB = 512
NHB = D // HB
E = 8
EPS = 1e-5

R_PH1 = 2 * NT
R_ATTN = 2 * (2 * NSA)
R_PHC = 2 * NT
A_TP_TOTAL = R_PH1 + R_ATTN + R_PHC


class Ctr:
    def __init__(self):
        self.v = 0

    def inc(self, n=1):
        self.v += n
        return self.v


def build_program():
    nc = bass.Bass("TRN2", target_bir_lowering=False)

    x_d = nc.dram_tensor("x", [T, D], F32, kind="ExternalInput").ap()
    wat_d = nc.dram_tensor("wat", [D, D], F32, kind="ExternalInput").ap()
    mask_d = nc.dram_tensor("mask", [D, D], I32, kind="ExternalInput").ap()
    wg_d = nc.dram_tensor("wg", [128, NK * E], F32, kind="ExternalInput").ap()
    bg_d = nc.dram_tensor("bg", [1, E], F32, kind="ExternalInput").ap()
    we_d = nc.dram_tensor("we", [E, D, D], F32, kind="ExternalInput").ap()
    be_d = nc.dram_tensor("be", [E, D], F32, kind="ExternalInput").ap()
    eyeb_d = nc.dram_tensor("eyeb", [128, 128], BF16, kind="ExternalInput").ap()
    a32_d = nc.dram_tensor("a32", [128, 128], BF16, kind="ExternalInput").ap()
    out_d = nc.dram_tensor("out", [T, D], F32, kind="ExternalOutput").ap()
    x1s_d = nc.dram_tensor("x1s", [T, D], F32, kind="Internal").ap()

    st = ExitStack()
    with st:
        lnT = st.enter_context(nc.sbuf_tensor("lnT", [128, NK * T], BF16))
        xt = st.enter_context(nc.sbuf_tensor("xt", [128, 2 * D], F32))
        lnbf = st.enter_context(nc.sbuf_tensor("lnbf", [128, D], BF16))
        eyeb = st.enter_context(nc.sbuf_tensor("eyeb_s", [128, 128], BF16))
        a32 = st.enter_context(nc.sbuf_tensor("a32_s", [128, 128], BF16))
        stats = st.enter_context(nc.sbuf_tensor("stats", [128, 64], F32))
        bnst = st.enter_context(nc.sbuf_tensor("bnst", [128, 48], F32))
        wgb = st.enter_context(nc.sbuf_tensor("wgb", [128, NK * E], BF16))
        bgs = st.enter_context(nc.sbuf_tensor("bgs", [128, E], F32))
        logits = st.enter_context(nc.sbuf_tensor("logits", [128, NT * E], F32))
        top8 = st.enter_context(nc.sbuf_tensor("top8", [128, E], F32))
        gsb = st.enter_context(nc.sbuf_tensor("gsb", [128, NT * E], F32))
        ps = st.enter_context(nc.psum_tensor("ps", [128, 8 * 512], F32))

        C_S1, C_S2, C_M, C_R, C_R1, C_R2, C_T = 0, 8, 16, 24, 32, 40, 48

        sems = {}
        ctrs = {}
        for name in (["s_one", "s_onep", "s_xt0", "s_xt1",
                      "s_mask0", "s_mask1", "s_wst0", "s_wst1",
                      "s_wr0", "s_wr1", "s_wrh0", "s_wrh1", "s_xth0", "s_xth1", "s_acc", "s_acc1", "s_bias", "s_out", "s_out1"]
                     + [f"s_xc{i}" for i in range(8)]
                     + [f"s_sp{i}" for i in range(8)] + [
                     "v_xt", "v_ln", "v_sqr", "v_st1", "v_x1", "v_hv", "v_wm",
                     "v_lg", "v_g", "v_pb", "v_aa", "v_rs",
                     "t_tp", "t_a32", "t_attn", "t_gate", "t_mm",
                     "a_tp", "a_sq", "a_st1", "a_relu"]):
            sems[name] = st.enter_context(nc.semaphore(name))
            ctrs[name] = Ctr()
        S = sems.__getitem__
        CN = ctrs.__getitem__

        block = st.enter_context(nc.Block())

        psb = ps[:, 4 * 512:6 * 512].bitcast(BF16)

        def prep_stripes(s):
            out = []
            if s == 0:
                out += [0, 1]
            if s < NSA - 1:
                out += [2 * (s + 1), 2 * (s + 1) + 1]
            return out

        stA = ExitStack()
        wmT = stA.enter_context(nc.sbuf_tensor("wmT", [128, 2 * NK * NB], BF16))
        wstr = stA.enter_context(nc.sbuf_tensor("wstr", [128, 2 * D], BF16))
        mstr = stA.enter_context(nc.sbuf_tensor("mstr", [128, 2 * D], BF16))
        rs = stA.enter_context(nc.sbuf_tensor("rs", [128, 2 * 128], BF16))
        bm01 = stA.enter_context(nc.sbuf_tensor("bm01", [128, 2 * 128], BF16))
        sqc = stA.enter_context(nc.sbuf_tensor("sqc", [128, 2 * 512], F32))
        xcb = stA.enter_context(nc.sbuf_tensor("xcb", [128, 8 * NB], F32))
        x1c = stA.enter_context(nc.sbuf_tensor("x1c", [128, 8 * NB], F32))

        @block.sync
        def _(sy):
            sy.dma_start(out=eyeb[:], in_=eyeb_d).then_inc(S("s_one"), 16)
            sy.dma_start(out=a32[:], in_=a32_d).then_inc(S("s_one"), 16)
            sy.dma_start(out=bgs[:], in_=bg_d.broadcast_to([128, E])).then_inc(S("s_one"), 16)
            CN("s_one").inc(48)
            for tt in range(NT):
                if tt >= 2:
                    sy.wait_ge(S("v_xt"), tt - 1)
                sy.dma_start(out=xt[:, (tt % 2) * D:(tt % 2 + 1) * D],
                             in_=x_d[tt * 128:(tt + 1) * 128, :]).then_inc(S(f"s_xt{tt % 2}"), 16)
            for s in range(NSA):
                for tt in range(NT):
                    idx = s * NT + tt
                    if idx >= 8:
                        sy.wait_ge(S("v_x1"), idx - 7)
                    sy.dma_start(out=xcb[:, (idx % 8) * NB:(idx % 8 + 1) * NB],
                                 in_=x_d[tt * 128:(tt + 1) * 128, s * NB:(s + 1) * NB]
                                 ).then_inc(S(f"s_xc{idx % 8}"), 16)
                if s >= 1:
                    for tt in range(NT):
                        idx = (s - 1) * NT + tt
                        sy.wait_ge(S("v_x1"), idx + 1)
                        sy.dma_start(out=x1s_d[tt * 128:(tt + 1) * 128, (s - 1) * NB:s * NB],
                                     in_=x1c[:, (idx % 8) * NB:(idx % 8 + 1) * NB]
                                     ).then_inc(S(f"s_sp{idx % 8}"), 16)
            for tt in range(NT):
                idx = (NSA - 1) * NT + tt
                sy.wait_ge(S("v_x1"), idx + 1)
                sy.dma_start(out=x1s_d[tt * 128:(tt + 1) * 128, (NSA - 1) * NB:NSA * NB],
                             in_=x1c[:, (idx % 8) * NB:(idx % 8 + 1) * NB]
                             ).then_inc(S(f"s_sp{idx % 8}"), 16)

        @block.gpsimd
        def _(g):
            g.dma_start(out=wgb[:], in_=wg_d).then_inc(S("s_onep"), 16)
            CN("s_onep").inc(16)
            for sti in range(2 * NSA):
                if sti >= 2:
                    g.wait_ge(S("v_rs"), sti - 1)
                    g.wait_ge(S("t_tp"), R_PH1 + (sti - 1) * 2)
                g.dma_start(out=mstr[:, (sti % 2) * D:(sti % 2 + 1) * D],
                            in_=mask_d[sti * 128:(sti + 1) * 128, :]).then_inc(S(f"s_mask{sti % 2}"), 16)
                g.dma_start(out=wstr[:, (sti % 2) * D:(sti % 2 + 1) * D],
                            in_=wat_d[sti * 128:(sti + 1) * 128, :]).then_inc(S(f"s_wst{sti % 2}"), 16)

        @block.vector
        def _(v):
            v.memset(stats[:, C_S1:C_S2 + 8], 0.0)
            v.drain()
            for tt in range(NT):
                xv = xt[:, (tt % 2) * D:(tt % 2 + 1) * D]
                v.wait_ge(S(f"s_xt{tt % 2}"), 16 * (tt // 2 + 1))
                for c in range(8):
                    v.bn_stats(out=bnst[:, c * 6:(c + 1) * 6],
                               in_=xv[:, c * 512:(c + 1) * 512])
                v.drain()
                v.bn_aggr(out=stats[:, C_T + 2 * tt:C_T + 2 * tt + 2], in_=bnst[:])
                v.drain()
                CN("v_st1").inc()
                v.tensor_scalar_add(out=stats[:, C_T + 2 * tt + 1:C_T + 2 * tt + 2],
                                    in0=stats[:, C_T + 2 * tt + 1:C_T + 2 * tt + 2],
                                    scalar1=EPS)
                v.drain().then_inc(S("v_st1"), 1)
                v.wait_ge(S("a_st1"), tt + 1)
                for h in range(2):
                    if tt >= 1:
                        v.wait_ge(S("t_tp"), 2 * tt)
                    CN("v_ln").inc()
                    v.tensor_scalar(out=lnbf[:, h * 2048:(h + 1) * 2048],
                                    in0=xv[:, h * 2048:(h + 1) * 2048],
                                    scalar1=stats[:, C_T + 2 * tt:C_T + 2 * tt + 1],
                                    scalar2=stats[:, C_R + tt:C_R + tt + 1],
                                    op0=ALU.subtract, op1=ALU.mult)
                    v.drain().then_inc(S("v_ln"), 1)
                CN("v_xt").inc()
                v.engine_nop().then_inc(S("v_xt"), 1)

            def emit_evac(s, tt):
                idx = s * NT + tt
                v.wait_ge(S("t_attn"), idx + 1)
                v.wait_ge(S(f"s_xc{idx % 8}"), 16 * (idx // 8 + 1))
                if idx >= 8:
                    v.wait_ge(S(f"s_sp{idx % 8}"), 16 * (idx // 8))
                    v.wait_ge(S("a_sq"), idx - 7)
                bank = tt % 4
                CN("v_x1").inc()
                v.tensor_tensor(out=x1c[:, (idx % 8) * NB:(idx % 8 + 1) * NB],
                                in0=ps[:, bank * 512:bank * 512 + NB],
                                in1=xcb[:, (idx % 8) * NB:(idx % 8 + 1) * NB],
                                op=ALU.add).then_inc(S("v_x1"), 1)
                v.drain()
                v.reduce_sum(out=stats[:, C_R1 + tt:C_R1 + tt + 1],
                             in_=x1c[:, (idx % 8) * NB:(idx % 8 + 1) * NB], axis=AX.X)

            def emit_rowsum(sti):
                v.wait_ge(S(f"s_mask{sti % 2}"), 16 * (sti // 2 + 1))
                CN("v_rs").inc()
                with nc.allow_low_precision(reason="exact small-int mask block sums"):
                    v.reduce_sum(out=rs[:, (sti % 2) * 128:(sti % 2 + 1) * 128].unsqueeze(2),
                                 in_=mstr[:, (sti % 2) * D:(sti % 2 + 1) * D].rearrange(
                                     "p (a b) -> p a b", b=32),
                                 axis=AX.X)
                v.drain().then_inc(S("v_rs"), 1)

            def emit_prep(sti):
                v.wait_ge(S("t_a32"), sti + 1)
                pbank = 6 + (sti % 2)
                CN("v_hv").inc()
                v.tensor_scalar(out=bm01[:, (sti % 2) * 128:(sti % 2 + 1) * 128],
                                in0=ps[:, pbank * 512:pbank * 512 + 128],
                                scalar1=0.0, scalar2=None,
                                op0=ALU.is_gt).then_inc(S("v_hv"), 1)
                v.drain()
                v.wait_ge(S(f"s_wst{sti % 2}"), 16 * (sti // 2 + 1))
                bmv = bm01[:, (sti % 2) * 128:(sti % 2 + 1) * 128].unsqueeze(2) \
                    .broadcast_to([128, 128, 32])
                CN("v_wm").inc()
                wv = wstr[:, (sti % 2) * D:(sti % 2 + 1) * D].rearrange(
                    "p (kb r) -> p kb r", r=32)
                v.tensor_tensor(out=wv, in0=wv, in1=bmv, op=ALU.mult)
                v.drain().then_inc(S("v_wm"), 1)

            for s in range(NSA):
                if s >= 1:
                    v.drain()
                    v.wait_ge(S("a_sq"), s * NT)
                    v.tensor_add(out=stats[:, C_S1:C_S1 + 8], in0=stats[:, C_S1:C_S1 + 8],
                                 in1=stats[:, C_R1:C_R1 + 8])
                    v.tensor_add(out=stats[:, C_S2:C_S2 + 8], in0=stats[:, C_S2:C_S2 + 8],
                                 in1=stats[:, C_R2:C_R2 + 8])
                stripes = prep_stripes(s)
                if s == 0:
                    for sti in (0, 1):
                        emit_rowsum(sti)
                        emit_prep(sti)
                    emit_rowsum(2)
                    emit_rowsum(3)
                    emit_evac(s, 0)
                    emit_prep(2)
                    emit_evac(s, 1)
                    emit_evac(s, 2)
                    emit_prep(3)
                    emit_evac(s, 3)
                    for tt in range(4, NT):
                        emit_evac(s, tt)
                elif stripes:
                    emit_rowsum(stripes[0])
                    emit_rowsum(stripes[1])
                    emit_evac(s, 0)
                    emit_prep(stripes[0])
                    emit_evac(s, 1)
                    emit_evac(s, 2)
                    emit_prep(stripes[1])
                    emit_evac(s, 3)
                    for tt in range(4, NT):
                        emit_evac(s, tt)
                else:
                    for tt in range(NT):
                        emit_evac(s, tt)
            v.wait_ge(S("a_sq"), NSA * NT)
            v.drain()
            v.tensor_add(out=stats[:, C_S1:C_S1 + 8], in0=stats[:, C_S1:C_S1 + 8],
                         in1=stats[:, C_R1:C_R1 + 8])
            v.tensor_add(out=stats[:, C_S2:C_S2 + 8], in0=stats[:, C_S2:C_S2 + 8],
                         in1=stats[:, C_R2:C_R2 + 8])
            v.drain()

        @block.scalar
        def _(sc):
            for tt in range(NT):
                sc.wait_ge(S("v_st1"), tt + 1)
                sc.activation(out=stats[:, C_R1 + tt:C_R1 + tt + 1],
                              in_=stats[:, C_T + 2 * tt + 1:C_T + 2 * tt + 2], func=AF.Ln)
                sc.drain()
                CN("a_st1").inc()
                sc.activation(out=stats[:, C_R + tt:C_R + tt + 1],
                              in_=stats[:, C_R1 + tt:C_R1 + tt + 1],
                              func=AF.Exp, scale=-0.5)
                sc.drain().then_inc(S("a_st1"), 1)
                for half in range(2):
                    rnd = 2 * tt + half
                    sc.wait_ge(S("t_tp"), rnd + 1)
                    dst = lnT[:].rearrange("p (kc t) -> p kc t", kc=NK)[
                        :, half * 16:(half + 1) * 16, tt * 128:(tt + 1) * 128]
                    CN("a_tp").inc()
                    sc.copy(out=dst, in_=psb[:].rearrange("p (i t) -> p i t", i=16))
                    sc.drain().then_inc(S("a_tp"), 1)

            def emit_evacwmT(sti):
                so = (sti // 2) % 2
                for half in range(2):
                    rnd = R_PH1 + sti * 2 + half
                    sc.wait_ge(S("t_tp"), rnd + 1)
                    dst = wmT[:, so * NK * NB:(so + 1) * NK * NB].rearrange(
                        "p (kc n) -> p kc n", kc=NK)[
                        :, half * 16:(half + 1) * 16,
                        (sti % 2) * 128:(sti % 2) * 128 + 128]
                    CN("a_tp").inc()
                    sc.copy(out=dst, in_=psb[:].rearrange("p (i t) -> p i t", i=16))
                    sc.drain().then_inc(S("a_tp"), 1)

            def emit_sq(s, tt):
                idx = s * NT + tt
                sc.wait_ge(S("v_x1"), idx + 1)
                CN("a_sq").inc()
                sc.activation(out=sqc[:, (idx % 2) * 512:(idx % 2) * 512 + NB],
                              in_=x1c[:, (idx % 8) * NB:(idx % 8 + 1) * NB],
                              func=AF.Square,
                              accum_out=stats[:, C_R2 + tt:C_R2 + tt + 1])
                sc.drain().then_inc(S("a_sq"), 1)

            for s in range(NSA):
                stripes = prep_stripes(s)
                if s == 0:
                    emit_evacwmT(0)
                    emit_evacwmT(1)
                    emit_sq(s, 0)
                    emit_sq(s, 1)
                    emit_evacwmT(2)
                    emit_sq(s, 2)
                    emit_sq(s, 3)
                    emit_evacwmT(3)
                    for tt in range(4, NT):
                        emit_sq(s, tt)
                elif stripes:
                    emit_sq(s, 0)
                    emit_sq(s, 1)
                    emit_sq(s, 2)
                    emit_sq(s, 3)
                    emit_evacwmT(stripes[0])
                    emit_sq(s, 4)
                    emit_sq(s, 5)
                    emit_evacwmT(stripes[1])
                    emit_sq(s, 6)
                    emit_sq(s, 7)
                else:
                    for tt in range(NT):
                        emit_sq(s, tt)

        @block.tensor
        def _(t):
            t.wait_ge(S("s_one"), 48)
            for tt in range(NT):
                for half in range(2):
                    t.wait_ge(S("v_ln"), 2 * tt + half + 1)
                    rnd = CN("t_tp").v
                    if rnd >= 1:
                        t.wait_ge(S("a_tp"), rnd)
                    for i in range(16):
                        kc = half * 16 + i
                        ins = t.transpose(out=psb[:, i * 128:(i + 1) * 128],
                                          in_=lnbf[:, kc * 128:(kc + 1) * 128],
                                          identity=eyeb[:])
                    CN("t_tp").inc()
                    ins.then_inc(S("t_tp"), 1)

            def emit_a32(sti):
                t.wait_ge(S("v_rs"), sti + 1)
                pbank = 6 + (sti % 2)
                if sti >= 2:
                    t.wait_ge(S("v_hv"), sti - 1)
                CN("t_a32").inc()
                t.matmul(out=ps[:, pbank * 512:pbank * 512 + 128],
                         lhsT=a32[:],
                         rhs=rs[:, (sti % 2) * 128:(sti % 2 + 1) * 128],
                         start=True, stop=True).then_inc(S("t_a32"), 1)

            def emit_tp_round(sti, half):
                if half == 0:
                    t.wait_ge(S("v_wm"), sti + 1)
                rnd = CN("t_tp").v
                t.wait_ge(S("a_tp"), rnd)
                for i in range(16):
                    kc = half * 16 + i
                    ins = t.transpose(
                        out=psb[:, i * 128:(i + 1) * 128],
                        in_=wstr[:, (sti % 2) * D + kc * 128:(sti % 2) * D + (kc + 1) * 128],
                        identity=eyeb[:])
                CN("t_tp").inc()
                ins.then_inc(S("t_tp"), 1)

            for sti in (0, 1):
                emit_a32(sti)
                emit_tp_round(sti, 0)
                emit_tp_round(sti, 1)
            for s in range(NSA):
                nxt = prep_stripes(s) if s > 0 else [2, 3]
                t.wait_ge(S("a_tp"), R_PH1 + 4 * (s + 1))
                so = s % 2

                def g(tt):
                    idx = s * NT + tt
                    bank = tt % 4
                    if idx >= 4:
                        t.wait_ge(S("v_x1"), idx - 3)
                    for kc in range(NK):
                        ins = t.matmul(
                            out=ps[:, bank * 512:bank * 512 + NB],
                            lhsT=lnT[:, kc * T + tt * 128:kc * T + (tt + 1) * 128],
                            rhs=wmT[:, so * NK * NB + kc * NB:so * NK * NB + (kc + 1) * NB],
                            start=(kc == 0), stop=(kc == NK - 1))
                    CN("t_attn").inc()
                    ins.then_inc(S("t_attn"), 1)

                g(0)
                if nxt:
                    emit_a32(nxt[0])
                g(1)
                if nxt:
                    emit_a32(nxt[1])
                g(2)
                g(3)
                g(4)
                if nxt:
                    emit_tp_round(nxt[0], 0)
                g(5)
                if nxt:
                    emit_tp_round(nxt[0], 1)
                g(6)
                if nxt:
                    emit_tp_round(nxt[1], 0)
                    emit_tp_round(nxt[1], 1)
                g(7)

        stA.close()
        wring = st.enter_context(nc.sbuf_tensor("wring", [128, 2 * NK * HB], BF16))
        acc = st.enter_context(nc.sbuf_tensor("acc", [128, NT * HB], F32))
        bbuf = st.enter_context(nc.sbuf_tensor("bbuf", [128, E * HB], F32))
        t2 = st.enter_context(nc.sbuf_tensor("t2", [128, 2 * HB], F32))
        NSL = NHB * E

        @block.gpsimd
        def _(g):
            for i in range(8):
                g.wait_ge(S(f"s_sp{i}"), 16 * 16)
            g.wait_ge(S("a_sq"), NSA * NT)
            for sl in range(NSL):
                hb, e = divmod(sl, E)
                if sl >= 2:
                    g.wait_ge(S("t_mm"), (sl - 1) * NT)
                for j in range(NK // 4):
                    src = we_d[e, j * 512:(j + 1) * 512, hb * HB:(hb + 1) * HB]
                    src = src.rearrange("(c p) h -> p c h", p=128)
                    dst = wring[:, (sl % 2) * NK * HB + j * 4 * HB:
                                (sl % 2) * NK * HB + (j * 4 + 4) * HB]
                    dst = dst.rearrange("p (c h) -> p c h", c=4)
                    sem = "s_wrh" if j < 4 else "s_wr"
                    g.dma_start(out=dst, in_=src).then_inc(S(f"{sem}{sl % 2}"), 16)

        @block.sync
        def _(sy):
            for tt in range(NT):
                sy.wait_ge(S(f"s_sp{tt}"), 16 * 16)
                if tt >= 2:
                    sy.wait_ge(S("v_xt"), NT + tt - 1)
                sy.dma_start(out=xt[:, (tt % 2) * D:(tt % 2) * D + 2048],
                             in_=x1s_d[tt * 128:(tt + 1) * 128, 0:2048]
                             ).then_inc(S(f"s_xth{tt % 2}"), 16)
                sy.dma_start(out=xt[:, (tt % 2) * D + 2048:(tt % 2 + 1) * D],
                             in_=x1s_d[tt * 128:(tt + 1) * 128, 2048:4096]
                             ).then_inc(S(f"s_xt{tt % 2}"), 16)
            for hb in range(NHB):
                for half in range(2):
                    if hb >= 1:
                        sy.wait_ge(S("s_out" if half == 0 else "s_out1"), 16 * hb)
                    sy.dma_start(out=acc[:, half * 4 * HB:(half + 1) * 4 * HB]
                                 .rearrange("p (t h) -> p t h", t=4),
                                 in_=x1s_d[half * 512:(half + 1) * 512, hb * HB:(hb + 1) * HB]
                                 .rearrange("(t p) h -> p t h", p=128)
                                 ).then_inc(S("s_acc" if half == 0 else "s_acc1"), 16)
                if hb >= 1:
                    sy.wait_ge(S("v_aa"), hb * E * NT)
                for e in range(E):
                    sy.dma_start(out=bbuf[:, e * HB:(e + 1) * HB],
                                 in_=be_d[e:e + 1, hb * HB:(hb + 1) * HB]
                                 .broadcast_to([128, HB])).then_inc(S("s_bias"), 16)
                    CN("s_bias").inc(16)
                sy.wait_ge(S("v_aa"), hb * E * NT + (E - 1) * NT + 4)
                sy.dma_start(out=out_d[0:512, hb * HB:(hb + 1) * HB].rearrange(
                    "(t p) h -> p t h", p=128),
                    in_=acc[:, 0:4 * HB].rearrange("p (t h) -> p t h", t=4)
                    ).then_inc(S("s_out"), 16)
                sy.wait_ge(S("v_aa"), (hb + 1) * E * NT)
                sy.dma_start(out=out_d[512:1024, hb * HB:(hb + 1) * HB].rearrange(
                    "(t p) h -> p t h", p=128),
                    in_=acc[:, 4 * HB:8 * HB].rearrange("p (t h) -> p t h", t=4)
                    ).then_inc(S("s_out1"), 16)
            sy.wait_ge(S("s_out"), 16 * NHB)
            sy.wait_ge(S("s_out1"), 16 * NHB)

        @block.vector
        def _(v):
            v.tensor_scalar_mul(out=stats[:, C_M:C_M + 8], in0=stats[:, C_S1:C_S1 + 8],
                                scalar1=1.0 / D)
            v.tensor_scalar_mul(out=stats[:, C_T:C_T + 8], in0=stats[:, C_S2:C_S2 + 8],
                                scalar1=1.0 / D)
            v.drain()
            v.tensor_mul(out=stats[:, C_T + 8:C_T + 16], in0=stats[:, C_M:C_M + 8],
                         in1=stats[:, C_M:C_M + 8])
            v.drain()
            v.tensor_sub(out=stats[:, C_T:C_T + 8], in0=stats[:, C_T:C_T + 8],
                         in1=stats[:, C_T + 8:C_T + 16])
            v.drain()
            CN("v_st1").inc()
            v.tensor_scalar_add(out=stats[:, C_T:C_T + 8], in0=stats[:, C_T:C_T + 8],
                                scalar1=EPS)
            v.drain().then_inc(S("v_st1"), 1)
            for tt in range(NT):
                v.wait_ge(S("a_st1"), NT + 1)
                for h in range(2):
                    if h == 0:
                        v.wait_ge(S(f"s_xth{tt % 2}"), 16 * (tt // 2 + 1))
                    else:
                        v.wait_ge(S(f"s_xt{tt % 2}"), 16 * (4 + tt // 2 + 1))
                    v.wait_ge(S("t_tp"), R_PH1 + R_ATTN + 2 * tt)
                    CN("v_ln").inc()
                    v.tensor_scalar(out=lnbf[:, h * 2048:(h + 1) * 2048],
                                    in0=xt[:, (tt % 2) * D + h * 2048:(tt % 2) * D + (h + 1) * 2048],
                                    scalar1=stats[:, C_M + tt:C_M + tt + 1],
                                    scalar2=stats[:, C_R + tt:C_R + tt + 1],
                                    op0=ALU.subtract, op1=ALU.mult)
                    v.drain().then_inc(S("v_ln"), 1)
                CN("v_xt").inc()
                v.engine_nop().then_inc(S("v_xt"), 1)
            for tt in range(NT):
                v.wait_ge(S("t_gate"), tt + 1)
                bank = 6 + (tt % 2)
                CN("v_lg").inc()
                v.tensor_tensor(out=logits[:, tt * E:(tt + 1) * E],
                                in0=ps[:, bank * 512 + (tt // 2) * 8:
                                       bank * 512 + (tt // 2) * 8 + 8],
                                in1=bgs[:], op=ALU.add)
                v.drain().then_inc(S("v_lg"), 1)
                v.max(out=top8[:], in_=logits[:, tt * E:(tt + 1) * E])
                v.drain()
                CN("v_g").inc()
                v.tensor_scalar(out=gsb[:, tt * E:(tt + 1) * E],
                                in0=logits[:, tt * E:(tt + 1) * E],
                                scalar1=top8[:, 1:2], scalar2=0.5,
                                op0=ALU.is_ge, op1=ALU.mult)
                v.drain().then_inc(S("v_g"), 1)
            for sl in range(NSL):
                hb, e = divmod(sl, E)
                if e == 0:
                    v.wait_ge(S("s_acc"), 16 * (hb + 1))
                    v.wait_ge(S("s_bias"), 16 * E * (hb + 1))
                for tt in range(NT):
                    n = sl * NT + tt
                    if e == 0 and tt == 4:
                        v.wait_ge(S("s_acc1"), 16 * (hb + 1))
                    v.wait_ge(S("t_mm"), n + 1)
                    CN("v_pb").inc()
                    v.tensor_tensor(out=ps[:, tt * 512:(tt + 1) * 512],
                                    in0=ps[:, tt * 512:(tt + 1) * 512],
                                    in1=bbuf[:, e * HB:(e + 1) * HB],
                                    op=ALU.add).then_inc(S("v_pb"), 1)
                    v.wait_ge(S("a_relu"), n + 1)
                    CN("v_aa").inc()
                    v.tensor_tensor(out=acc[:, tt * HB:(tt + 1) * HB],
                                    in0=acc[:, tt * HB:(tt + 1) * HB],
                                    in1=t2[:, (n % 2) * HB:(n % 2 + 1) * HB],
                                    op=ALU.add).then_inc(S("v_aa"), 1)

        @block.scalar
        def _(sc):
            sc.wait_ge(S("v_st1"), NT + 1)
            sc.activation(out=stats[:, C_T + 8:C_T + 16], in_=stats[:, C_T:C_T + 8],
                          func=AF.Ln)
            sc.drain()
            CN("a_st1").inc()
            sc.activation(out=stats[:, C_R:C_R + 8], in_=stats[:, C_T + 8:C_T + 16],
                          func=AF.Exp, scale=-0.5)
            sc.drain().then_inc(S("a_st1"), 1)
            for rnd in range(2 * NT):
                tt, half = divmod(rnd, 2)
                sc.wait_ge(S("t_tp"), R_PH1 + R_ATTN + rnd + 1)
                dst = lnT[:].rearrange("p (kc t) -> p kc t", kc=NK)[
                    :, half * 16:(half + 1) * 16, tt * 128:(tt + 1) * 128]
                CN("a_tp").inc()
                sc.copy(out=dst, in_=psb[:].rearrange("p (i t) -> p i t", i=16))
                sc.drain().then_inc(S("a_tp"), 1)
            for sl in range(NSL):
                hb, e = divmod(sl, E)
                for tt in range(NT):
                    n = sl * NT + tt
                    if sl == 0:
                        sc.wait_ge(S("v_g"), tt + 1)
                    sc.wait_ge(S("v_pb"), n + 1)
                    if n >= 2:
                        sc.wait_ge(S("v_aa"), n - 1)
                    CN("a_relu").inc()
                    sc.activation(out=t2[:, (n % 2) * HB:(n % 2 + 1) * HB],
                                  in_=ps[:, tt * 512:(tt + 1) * 512], func=AF.Relu,
                                  scale=gsb[:, tt * E + e:tt * E + e + 1])
                    sc.drain().then_inc(S("a_relu"), 1)

        @block.tensor
        def _(t):
            t.wait_ge(S("v_x1"), NSA * NT)
            t.wait_ge(S("s_onep"), 16)
            for tt in range(NT):
                for half in range(2):
                    t.wait_ge(S("v_ln"), 2 * NT + 2 * tt + half + 1)
                    rnd = CN("t_tp").v
                    t.wait_ge(S("a_tp"), rnd)
                    for i in range(16):
                        kc = half * 16 + i
                        ins = t.transpose(out=psb[:, i * 128:(i + 1) * 128],
                                          in_=lnbf[:, kc * 128:(kc + 1) * 128],
                                          identity=eyeb[:])
                    CN("t_tp").inc()
                    ins.then_inc(S("t_tp"), 1)
            t.wait_ge(S("a_tp"), A_TP_TOTAL)
            for tt in range(NT):
                bank = 6 + (tt % 2)
                if tt >= 2:
                    t.wait_ge(S("v_lg"), tt - 1)
                for kc in range(NK):
                    ins = t.matmul(
                        out=ps[:, bank * 512 + (tt // 2) * 8:bank * 512 + (tt // 2) * 8 + 8],
                        lhsT=lnT[:, kc * T + tt * 128:kc * T + (tt + 1) * 128],
                        rhs=wgb[:, kc * E:(kc + 1) * E],
                        start=(kc == 0), stop=(kc == NK - 1))
                CN("t_gate").inc()
                ins.then_inc(S("t_gate"), 1)
                if tt < 4:
                    if tt == 0:
                        t.wait_ge(S("s_wrh0"), 16 * 4)
                    for kc in range(NK):
                        if tt == 0 and kc == 16:
                            t.wait_ge(S("s_wr0"), 16 * 4)
                        ins = t.matmul(
                            out=ps[:, tt * 512:(tt + 1) * 512],
                            lhsT=lnT[:, kc * T + tt * 128:kc * T + (tt + 1) * 128],
                            rhs=wring[:, kc * HB:(kc + 1) * HB],
                            start=(kc == 0), stop=(kc == NK - 1))
                    CN("t_mm").inc()
                    ins.then_inc(S("t_mm"), 1)
            for sl in range(NSL):
                hb, e = divmod(sl, E)
                t.wait_ge(S(f"s_wrh{sl % 2}"), 16 * 4 * (sl // 2 + 1))
                first_full = [True]
                for tt in range(NT):
                    if sl == 0 and tt < 4:
                        continue
                    n = sl * NT + tt
                    if sl == 0 and tt == 4:
                        t.wait_ge(S("a_tp"), A_TP_TOTAL)
                    if sl == 0 and tt >= 6:
                        t.wait_ge(S("v_lg"), tt + 1)
                    if sl >= 1:
                        t.wait_ge(S("a_relu"), n + 1 - NT)
                    for kc in range(NK):
                        if kc == 16 and first_full[0]:
                            t.wait_ge(S(f"s_wr{sl % 2}"), 16 * 4 * (sl // 2 + 1))
                            first_full[0] = False
                        ins = t.matmul(
                            out=ps[:, tt * 512:(tt + 1) * 512],
                            lhsT=lnT[:, kc * T + tt * 128:kc * T + (tt + 1) * 128],
                            rhs=wring[:, (sl % 2) * NK * HB + kc * HB:
                                      (sl % 2) * NK * HB + (kc + 1) * HB],
                            start=(kc == 0), stop=(kc == NK - 1))
                    CN("t_mm").inc()
                    ins.then_inc(S("t_mm"), 1)

    return nc


def kernel(L0_attn_data, L0_attn_weight, L0_attn_mask, L0_ffn_W_gate,
           L0_ffn_b_gate, L0_ffn_W_e, L0_ffn_b_e):
    x = np.ascontiguousarray(np.asarray(L0_attn_data, dtype=np.float32).reshape(8 * T, D))
    wat = np.ascontiguousarray(np.asarray(L0_attn_weight, dtype=np.float32))
    mask = np.ascontiguousarray(np.asarray(L0_attn_mask, dtype=np.int32))
    wg = np.asarray(L0_ffn_W_gate, dtype=np.float32)
    wg = np.ascontiguousarray(wg.reshape(NK, 128, E).transpose(1, 0, 2).reshape(128, NK * E))
    bg = np.ascontiguousarray(np.asarray(L0_ffn_b_gate, dtype=np.float32).reshape(1, E))
    we = np.ascontiguousarray(np.asarray(L0_ffn_W_e, dtype=np.float32))
    be = np.ascontiguousarray(np.asarray(L0_ffn_b_e, dtype=np.float32))
    eyeb = np.eye(128, dtype=np.float32).astype(ml_dtypes.bfloat16)
    a32 = (np.arange(128)[:, None] // 32 == np.arange(128)[None, :] // 32)
    a32 = a32.astype(np.float32).astype(ml_dtypes.bfloat16)

    nc = build_program()
    in_maps = []
    for c in range(8):
        in_maps.append(dict(x=x[c * T:(c + 1) * T], wat=wat, mask=mask, wg=wg, bg=bg,
                            we=we, be=be, eyeb=eyeb, a32=a32))
    res = run_bass_kernel_spmd(nc, in_maps, core_ids=list(range(8)))
    out = np.concatenate([r["out"] for r in res.results], axis=0)
    return out.reshape(4, 2048, D)


if __name__ == "__main__":
    import reference as _r
    inputs = {k: np.asarray(v) for k, v in _r.setup_inputs().items()}
    got = kernel(**inputs)
    exp = np.asarray(_r.reference(**inputs))
    print("resvar:", ((got - exp) ** 2).mean() / (exp ** 2).mean())
